# version 01
# baseline (speedup 1.0000x reference)
import numpy as np
import ml_dtypes
from contextlib import ExitStack
import concourse.bass as bass
import concourse.mybir as mybir
from concourse.bass_utils import run_bass_kernel_spmd

F32 = mybir.dt.float32
BF16 = mybir.dt.bfloat16
AF = mybir.ActivationFunctionType
ALU = mybir.AluOpType

B, T, C = 2, 16384, 2048
NCORE = 8
TOK = 4096
WARM = 128
TTOT = WARM + TOK
NCC = 16
NOC = 64
STS = [(0, WARM), (WARM, 2048), (WARM + 2048, 2048)]
TSUB = 512
HALF = 1024

FORCE_SAME_NONPE = True

ENG_ATTR = {"pe": "tensor", "dve": "vector", "act": "scalar", "pool": "gpsimd", "sp": "sync"}
Q_ENG = {"qs": "sp", "qg": "pool"}
Q_NSEM = {"qs": 3, "qg": 2}


class Sched:
    def __init__(self):
        self.ops = []
        self.regions = {}

    def add(self, eng, build, r=(), w=(), dma=None, scan=False):
        deps = set()
        for key in r:
            reg = self.regions.get(key)
            if reg is not None and reg[0] is not None:
                deps.add(reg[0])
        for key in w:
            reg = self.regions.get(key)
            if reg is not None:
                if reg[0] is not None:
                    deps.add(reg[0])
                deps.update(reg[1])
        oid = len(self.ops)
        self.ops.append({"id": oid, "eng": eng, "build": build, "deps": deps,
                         "dma": dma, "scan": scan, "inc": False, "val": None,
                         "sem": None, "waits": None})
        for key in r:
            reg = self.regions.get(key)
            if reg is None:
                self.regions[key] = (None, [oid])
            else:
                reg[1].append(oid)
        for key in w:
            self.regions[key] = (oid, [])
        return oid

    def finalize(self):
        ops = self.ops
        for op in ops:
            eng = Q_ENG[op["dma"]] if op["dma"] else op["eng"]
            waits = []
            for d in op["deps"]:
                p = ops[d]
                peng = Q_ENG[p["dma"]] if p["dma"] else p["eng"]
                if p["dma"] is not None:
                    need = True
                elif peng != eng:
                    need = True
                elif p["scan"]:
                    need = True
                elif FORCE_SAME_NONPE and peng != "pe":
                    need = True
                else:
                    need = False
                if need:
                    p["inc"] = True
                    waits.append(d)
            op["waits"] = waits
        ecount = {k: 0 for k in ENG_ATTR}
        qcount = {"qs": 0, "qg": 0}
        for op in ops:
            if op["dma"] is not None:
                q = op["dma"]
                i = qcount[q]
                qcount[q] += 1
                ns = Q_NSEM[q]
                op["sem"] = (q, i % ns)
                op["val"] = 16 * (i // ns + 1)
            elif op["inc"]:
                en = op["eng"]
                ecount[en] += 1
                op["sem"] = ("e", en)
                op["val"] = ecount[en]
        self.qfinal = {}
        for q, n in qcount.items():
            ns = Q_NSEM[q]
            self.qfinal[q] = [16 * ((n - 1 - j) // ns + 1) if n > j else 0
                              for j in range(ns)]


def sylvester(n):
    h = np.array([[1]], dtype=np.int8)
    while h.shape[0] < n:
        h = np.block([[h, h], [h, -h]])
    return h


def build_nc():
    nc = bass.Bass("TRN2", target_bir_lowering=False, debug=False)
    ap = lambda t: t.ap()

    xD = ap(nc.dram_tensor("x", [C, TTOT], F32, kind="ExternalInput"))
    wrvkD = ap(nc.dram_tensor("wrvk", [48, 128, NCC, 128], F32, kind="ExternalInput"))
    wupD = ap(nc.dram_tensor("wup", [NOC, 128, NCC, 128], F32, kind="ExternalInput"))
    wgateD = ap(nc.dram_tensor("wgate", [NOC, 128, NCC, 128], F32, kind="ExternalInput"))
    woutD = ap(nc.dram_tensor("wout", [NCC, 128, NCC, 128], F32, kind="ExternalInput"))
    hD = ap(nc.dram_tensor("hmat", [NCC, 128, NCC, 128], BF16, kind="ExternalInput"))
    wdplD = ap(nc.dram_tensor("wdpl", [128, 64 * 2048], F32, kind="ExternalInput"))
    h64D = ap(nc.dram_tensor("h64", [64, 64], BF16, kind="ExternalInput"))
    h128D = ap(nc.dram_tensor("h128", [128, 128], BF16, kind="ExternalInput"))
    tdD = ap(nc.dram_tensor("tdv", [128, NCC], F32, kind="ExternalInput"))
    tfD = ap(nc.dram_tensor("tfv", [128, NCC], F32, kind="ExternalInput"))
    suD = ap(nc.dram_tensor("suv", [128, NCC], F32, kind="ExternalInput"))
    wsuD = ap(nc.dram_tensor("wsuv", [128, NOC], F32, kind="ExternalInput"))
    wsgD = ap(nc.dram_tensor("wsgv", [128, NOC], F32, kind="ExternalInput"))
    wsdD = ap(nc.dram_tensor("wsdv", [128, NCC], F32, kind="ExternalInput"))
    sudD = ap(nc.dram_tensor("sudv", [128, NOC], F32, kind="ExternalInput"))
    wmaskD = ap(nc.dram_tensor("wmask", [128, 1], F32, kind="ExternalInput"))
    onesbD = ap(nc.dram_tensor("onesb", [128, 1], BF16, kind="ExternalInput"))
    onesrD = ap(nc.dram_tensor("onesr", [1, 128], F32, kind="ExternalInput"))
    outD = ap(nc.dram_tensor("out", [C, TOK], F32, kind="ExternalOutput"))
    partD = ap(nc.dram_tensor("part", [C, TOK], F32, kind="Internal"))
    gD = ap(nc.dram_tensor("gbuf", [8, 128, NOC * 512], BF16, kind="Internal"))
    a1D = ap(nc.dram_tensor("a1s", [128, 64 * 2048], BF16, kind="Internal"))
    kfD = ap(nc.dram_tensor("kf", [NCC, 128, NOC * 128], BF16, kind="Internal"))

    S = Sched()
    st_ = ExitStack()
    e = st_.enter_context

    RA = e(nc.sbuf_tensor("RA", [128, 32768], BF16))
    RB = e(nc.sbuf_tensor("RB", [128, 32768], BF16))
    EW = e(nc.sbuf_tensor("EW", [128, HALF], F32))
    SE = e(nc.sbuf_tensor("SE", [128, HALF], F32))
    EV = e(nc.sbuf_tensor("EV", [128, HALF], F32))
    VT = e(nc.sbuf_tensor("VT", [128, HALF], F32))
    SR = e(nc.sbuf_tensor("SR", [128, HALF], F32))
    SQb = e(nc.sbuf_tensor("SQb", [128, HALF + 1], F32))
    SVb = e(nc.sbuf_tensor("SVb", [128, HALF + 1], F32))
    STG = [e(nc.sbuf_tensor(f"STG{i}", [128, TSUB], F32)) for i in range(6)]
    XBF = [e(nc.sbuf_tensor(f"XBF{i}", [128, TSUB], BF16)) for i in range(2)]
    GBF = [e(nc.sbuf_tensor(f"GBF{i}", [128, 512], BF16)) for i in range(2)]
    SQB = GBF
    WRG = [e(nc.sbuf_tensor(f"WRG{i}", [128, 2048], BF16)) for i in range(2)]
    KST = [e(nc.sbuf_tensor(f"KST{i}", [128, 2048], BF16)) for i in range(2)]
    H64s = e(nc.sbuf_tensor("H64s", [64, 64], BF16))
    H128s = e(nc.sbuf_tensor("H128s", [128, 128], BF16))
    ROWS = e(nc.sbuf_tensor("ROWS", [1, 4 * TSUB], F32))
    TD = e(nc.sbuf_tensor("TD", [128, NCC], F32))
    TFs = e(nc.sbuf_tensor("TFs", [128, NCC], F32))
    WT = e(nc.sbuf_tensor("WT", [128, NCC], F32))
    EWT = e(nc.sbuf_tensor("EWT", [128, NCC], F32))
    EUT = e(nc.sbuf_tensor("EUT", [128, NCC], F32))
    SUT = e(nc.sbuf_tensor("SUT", [128, NCC], F32))
    WSU = e(nc.sbuf_tensor("WSU", [128, NOC], F32))
    WSG = e(nc.sbuf_tensor("WSG", [128, NOC], F32))
    WSD = e(nc.sbuf_tensor("WSD", [128, NCC], F32))
    SUDs = e(nc.sbuf_tensor("SUDs", [128, NOC], F32))
    WMK = e(nc.sbuf_tensor("WMK", [128, 1], F32))
    ONEB = e(nc.sbuf_tensor("ONEB", [128, 1], BF16))
    ONER = e(nc.sbuf_tensor("ONER", [1, 128], F32))
    CQ = e(nc.sbuf_tensor("CQ", [128, NCC], F32))
    CV = e(nc.sbuf_tensor("CV", [128, NCC], F32))
    ZCOL = e(nc.sbuf_tensor("ZCOL", [128, 1], F32))
    SC1 = e(nc.sbuf_tensor("SC1", [128, 1], F32))
    EPSC = e(nc.sbuf_tensor("EPSC", [128, 1], F32))
    NEGH = e(nc.sbuf_tensor("NEGH", [128, 1], F32))
    INVC = e(nc.sbuf_tensor("INVC", [128, 1], F32))
    PS = [e(nc.psum_tensor(f"PS{i}", [128, 512], F32)) for i in range(8)]

    sems = {}
    for name in ENG_ATTR:
        sems[("e", name)] = e(nc.semaphore(f"se_{name}"))
    for q, n in Q_NSEM.items():
        for j in range(n):
            sems[(q, j)] = e(nc.semaphore(f"sq_{q}{j}"))

    ps_ctr = [0]

    def psum():
        i = ps_ctr[0] % 8
        ps_ctr[0] += 1
        return PS[i], ("ps", i)

    def gr(name, off, nel):
        return [(name, g) for g in range(off // 512, (off + nel - 1) // 512 + 1)]

    def dma(q, dst, src, r=(), w=()):
        return S.add(None, lambda E, d=dst, s=src: E.dma_start(out=d, in_=s),
                     r=r, w=w, dma=q)

    dma("qs", TD[:], tdD, w=[("TD",)])
    dma("qs", TFs[:], tfD, w=[("TF",)])
    dma("qs", SUT[:], suD, w=[("SUT",)])
    dma("qs", WSU[:], wsuD, w=[("WSU",)])
    dma("qs", WSG[:], wsgD, w=[("WSG",)])
    dma("qs", WSD[:], wsdD, w=[("WSD",)])
    dma("qs", SUDs[:], sudD, w=[("SUD",)])
    dma("qs", WMK[:], wmaskD, w=[("WMK",)])
    dma("qs", ONEB[:], onesbD, w=[("ONEB",)])
    dma("qs", ONER[:], onesrD, w=[("ONER",)])
    S.add("act", lambda E: E.activation(out=WT[:], in_=TD[:], func=AF.Exp),
          r=[("TD",)], w=[("WT",)])
    S.add("dve", lambda E: E.tensor_scalar_mul(out=WT[:], in0=WT[:], scalar1=-1.0),
          r=[("WT",)], w=[("WT",)])
    S.add("act", lambda E: E.activation(out=EWT[:], in_=WT[:], func=AF.Exp),
          r=[("WT",)], w=[("EWT",)])
    S.add("act", lambda E: E.activation(out=EUT[:], in_=TFs[:], func=AF.Exp),
          r=[("TF",)], w=[("EUT",)])
    S.add("dve", lambda E: E.tensor_scalar_mul(out=WSU[:], in0=WSU[:], scalar1=64.0),
          r=[("WSU",)], w=[("WSU",)])
    S.add("dve", lambda E: E.tensor_scalar_mul(out=WSG[:], in0=WSG[:], scalar1=64.0),
          r=[("WSG",)], w=[("WSG",)])
    S.add("dve", lambda E: E.tensor_scalar_mul(out=WSD[:], in0=WSD[:],
                                               scalar1=float(1.0 / np.sqrt(8192.0))),
          r=[("WSD",)], w=[("WSD",)])
    S.add("dve", lambda E: E.memset(CQ[:], 0.0), w=[("cq", cc) for cc in range(NCC)])
    S.add("dve", lambda E: E.memset(CV[:], 0.0), w=[("cv", cc) for cc in range(NCC)])
    S.add("dve", lambda E: E.memset(ZCOL[:], 0.0), w=[("ZCOL",)])
    S.add("dve", lambda E: E.memset(SC1[:], float(1.0 / (64.0 * np.sqrt(2048.0)))), w=[("SC1",)])
    S.add("dve", lambda E: E.memset(EPSC[:], 1e-5), w=[("EPSC",)])
    S.add("dve", lambda E: E.memset(NEGH[:], -0.5), w=[("NEGH",)])
    S.add("dve", lambda E: E.memset(INVC[:], float(1.0 / 2048.0)), w=[("INVC",)])
    S.add("dve", lambda E: E.memset(EW[:], 0.0), w=[("EW",)])

    xR = xD.rearrange("(cc p) t -> p cc t", p=128)
    wrg_ctr = [0]

    def wcol_load(srcD, oc, cast):
        i = wrg_ctr[0] % 2
        wrg_ctr[0] += 1
        src = srcD[oc, :, :, :]
        dst = WRG[i][:].rearrange("p (kc c) -> p kc c", kc=NCC)
        dma("qg" if cast else "qs", dst, src, w=[("wrg", i)])
        return i

    def ln_stage(st):
        t0, L = STS[st]
        nts = max(1, L // TSUB)
        tsz = min(L, TSUB)
        for ts in range(nts):
            a = t0 + ts * tsz
            pm, pmk = psum()
            pq, pqk = psum()
            for cc in range(NCC):
                xb, xbk = XBF[cc % 2], ("xbf", cc % 2)
                dma("qg", xb[:, 0:tsz], xR[:, cc, a:a + tsz], w=[xbk])
                sb, sbk = SQB[cc % 2], ("gbf", cc % 2)
                S.add("act", lambda E, o=sb, i_=xb, n=tsz: E.activation(
                    out=o[:, 0:n], in_=i_[:, 0:n], func=AF.Square),
                    r=[xbk], w=[sbk])
                S.add("pe", lambda E, p=pm, i_=xb, n=tsz, s=(cc == 0), t=(cc == NCC - 1):
                      E.matmul(out=p[0:1, 0:n], lhsT=ONEB[:], rhs=i_[:, 0:n],
                               start=s, stop=t, skip_group_check=True),
                      r=[xbk, ("ONEB",)], w=[pmk])
                S.add("pe", lambda E, p=pq, i_=sb, n=tsz, s=(cc == 0), t=(cc == NCC - 1):
                      E.matmul(out=p[0:1, 0:n], lhsT=ONEB[:], rhs=i_[:, 0:n],
                               start=s, stop=t, skip_group_check=True),
                      r=[sbk, ("ONEB",)], w=[pqk])
            S.add("act", lambda E, n=tsz, p=pm: E.activation(
                out=ROWS[0:1, TSUB:TSUB + n], in_=p[0:1, 0:n], func=AF.Square,
                scale=INVC[0:1, :]),
                r=[pmk, ("INVC",)], w=[("r_mq",)])
            S.add("dve", lambda E, n=tsz, p=pq: E.scalar_tensor_tensor(
                out=ROWS[0:1, 0:n], in0=p[0:1, 0:n], scalar=float(1.0 / 2048.0),
                in1=ROWS[0:1, TSUB:TSUB + n], op0=ALU.mult, op1=ALU.subtract),
                r=[pqk, ("r_mq",)], w=[("r_vr",)])
            S.add("act", lambda E, n=tsz: E.activation(
                out=ROWS[0:1, 0:n], in_=ROWS[0:1, 0:n], func=AF.Ln,
                bias=EPSC[0:1, :]),
                r=[("r_vr",), ("EPSC",)], w=[("r_vr",)])
            S.add("act", lambda E, n=tsz: E.activation(
                out=ROWS[0:1, 2 * TSUB:2 * TSUB + n], in_=ROWS[0:1, 0:n], func=AF.Exp,
                scale=NEGH[0:1, :]),
                r=[("r_vr",), ("NEGH",)], w=[("r_rs",)])
            S.add("dve", lambda E, n=tsz, p=pm: E.scalar_tensor_tensor(
                out=ROWS[0:1, 3 * TSUB:3 * TSUB + n], in0=p[0:1, 0:n],
                scalar=float(-1.0 / 2048.0), in1=ROWS[0:1, 2 * TSUB:2 * TSUB + n],
                op0=ALU.mult, op1=ALU.mult),
                r=[pmk, ("r_rs",)], w=[("r_br",)])
            pa, pak = psum()
            pb, pbk = psum()
            S.add("pe", lambda E, n=tsz, p=pa: E.matmul(
                out=p[:, 0:n], lhsT=ONER[:], rhs=ROWS[0:1, 2 * TSUB:2 * TSUB + n],
                start=True, stop=True),
                r=[("r_rs",), ("ONER",)], w=[pak])
            S.add("pe", lambda E, n=tsz, p=pb: E.matmul(
                out=p[:, 0:n], lhsT=ONER[:], rhs=ROWS[0:1, 3 * TSUB:3 * TSUB + n],
                start=True, stop=True),
                r=[("r_br",), ("ONER",)], w=[pbk])
            for cc in range(NCC):
                li = cc % 2
                lx = SR[:, li * 512:li * 512 + tsz]
                lxk = ("sr", li)
                dma("qs", lx, xR[:, cc, a:a + tsz], w=[lxk])
                S.add("dve", lambda E, n=tsz, l=lx, p=pa: E.tensor_tensor(
                    out=l, in0=l, in1=p[:, 0:n], op=ALU.mult),
                    r=[lxk, pak], w=[lxk])
                off = cc * 2048 + ts * tsz
                S.add("dve", lambda E, n=tsz, l=lx, p=pb, o=off: E.tensor_tensor(
                    out=RA[:, o:o + n], in0=l, in1=p[:, 0:n], op=ALU.add),
                    r=[lxk, pbk], w=gr("RA", off, tsz))

    def wkv_stage(st):
        t0, L = STS[st]
        warm = (st == 0)
        nh = max(1, L // HALF)
        hsz = min(L, HALF)
        ntsub = max(1, hsz // TSUB)
        for h in range(nh):
            ha = t0 + h * hsz
            for cc in range(NCC):
                parts = [32 + cc, 16 + cc] + ([] if warm else [cc])
                pbanks = {}
                for oi, och in enumerate(parts):
                    wi = wcol_load(wrvkD, och, True)
                    for tsu in range(ntsub):
                        tsz = min(hsz, TSUB)
                        pp, ppk = psum()
                        pbanks[(oi, tsu)] = (pp, ppk)
                        for kk in range(NCC):
                            off = kk * 2048 + (ha - t0) + tsu * tsz
                            S.add("pe", lambda E, p=pp, w_=WRG[wi], k2=kk, o=off, n=tsz,
                                  s=(kk == 0), t=(kk == NCC - 1):
                                  E.matmul(out=p[:, 0:n],
                                           lhsT=w_[:, k2 * 128:(k2 + 1) * 128],
                                           rhs=RA[:, o:o + n], start=s, stop=t),
                                  r=[("wrg", wi)] + gr("RA", off, tsz), w=[ppk])
                for tsu in range(ntsub):
                    tsz = min(hsz, TSUB)
                    pp, ppk = pbanks[(0, tsu)]
                    S.add("act", lambda E, p=pp, o=tsu * TSUB, n=tsz: E.activation(
                        out=SE[:, o:o + n], in_=p[:, 0:n], func=AF.Exp),
                        r=[ppk], w=[("SE",)])
                    pp, ppk = pbanks[(1, tsu)]
                    S.add("act", lambda E, p=pp, o=tsu * TSUB, n=tsz: E.copy(
                        out=VT[:, o:o + n], in_=p[:, 0:n]),
                        r=[ppk], w=[("VT",)])
                    if not warm:
                        pp, ppk = pbanks[(2, tsu)]
                        S.add("act", lambda E, p=pp, o=tsu * TSUB, n=tsz: E.activation(
                            out=SR[:, o:o + n], in_=p[:, 0:n], func=AF.Sigmoid),
                            r=[ppk], w=[("sr", tsu)])
                if warm:
                    S.add("dve", lambda E, n=hsz: E.tensor_scalar_mul(
                        out=SE[:, 0:n], in0=SE[:, 0:n], scalar1=WMK[:]),
                        r=[("SE",), ("WMK",)], w=[("SE",)])
                S.add("act", lambda E, c=cc, n=hsz: E.activation(
                    out=EW[:, 0:n], in_=EW[:, 0:n], func=AF.Identity,
                    scale=ZCOL[:], bias=EWT[:, c:c + 1]),
                    r=[("EW",), ("ZCOL",), ("EWT",)], w=[("EW",)])
                S.add("pool", lambda E, n=hsz: E.tensor_tensor(
                    out=EV[:, 0:n], in0=SE[:, 0:n], in1=VT[:, 0:n], op=ALU.mult),
                    r=[("SE",), ("VT",)], w=[("EVr",)])
                S.add("dve", lambda E, c=cc, n=hsz: E.tensor_copy(
                    out=SQb[:, 0:1], in_=CQ[:, c:c + 1]),
                    r=[("cq", cc)], w=[("SQ0",)])
                S.add("dve", lambda E, c=cc, n=hsz: E.tensor_copy(
                    out=SVb[:, 0:1], in_=CV[:, c:c + 1]),
                    r=[("cv", cc)], w=[("SV0",)])
                S.add("dve", lambda E, c=cc, n=hsz: E.tensor_tensor_scan(
                    out=SQb[:, 1:n + 1], data0=EW[:, 0:n], data1=SE[:, 0:n],
                    initial=CQ[:, c:c + 1], op0=ALU.mult, op1=ALU.add),
                    r=[("EW",), ("SE",), ("cq", cc)], w=[("SQ",)], scan=True)
                S.add("dve", lambda E, c=cc, n=hsz: E.tensor_tensor_scan(
                    out=SVb[:, 1:n + 1], data0=EW[:, 0:n], data1=EV[:, 0:n],
                    initial=CV[:, c:c + 1], op0=ALU.mult, op1=ALU.add),
                    r=[("EW",), ("EVr",), ("cv", cc)], w=[("SV",)], scan=True)
                S.add("dve", lambda E, c=cc, n=hsz: E.tensor_copy(
                    out=CQ[:, c:c + 1], in_=SQb[:, n:n + 1]),
                    r=[("SQ",)], w=[("cq", cc)])
                S.add("dve", lambda E, c=cc, n=hsz: E.tensor_copy(
                    out=CV[:, c:c + 1], in_=SVb[:, n:n + 1]),
                    r=[("SV",)], w=[("cv", cc)])
                if warm:
                    continue
                S.add("dve", lambda E, c=cc, n=hsz: E.tensor_scalar_mul(
                    out=SE[:, 0:n], in0=SE[:, 0:n], scalar1=EUT[:, c:c + 1]),
                    r=[("SE",), ("EUT",)], w=[("SE",)])
                S.add("dve", lambda E, n=hsz: E.tensor_tensor(
                    out=VT[:, 0:n], in0=VT[:, 0:n], in1=SE[:, 0:n], op=ALU.mult),
                    r=[("VT",), ("SE",)], w=[("VT",)])
                S.add("pool", lambda E, n=hsz: E.tensor_tensor(
                    out=VT[:, 0:n], in0=VT[:, 0:n], in1=SVb[:, 0:n], op=ALU.add),
                    r=[("VT",), ("SV",), ("SV0",)], w=[("VT",)])
                S.add("pool", lambda E, n=hsz: E.tensor_tensor(
                    out=SE[:, 0:n], in0=SE[:, 0:n], in1=SQb[:, 0:n], op=ALU.add),
                    r=[("SE",), ("SQ",), ("SQ0",)], w=[("SE",)])
                S.add("dve", lambda E, n=hsz: E.reciprocal(
                    out=EW[:, 0:n], in_=SE[:, 0:n]),
                    r=[("SE",)], w=[("EW",)])
                S.add("dve", lambda E, n=hsz: E.tensor_tensor(
                    out=VT[:, 0:n], in0=VT[:, 0:n], in1=EW[:, 0:n], op=ALU.mult),
                    r=[("VT",), ("EW",)], w=[("VT",)])
                off = cc * 2048 + (ha - t0)
                S.add("dve", lambda E, n=hsz, o=off: E.tensor_tensor(
                    out=RB[:, o:o + n], in0=VT[:, 0:n], in1=SR[:, 0:n], op=ALU.mult),
                    r=[("VT",), ("sr", 0), ("sr", 1)], w=gr("RB", off, hsz))

    def wout_stage(st):
        t0, L = STS[st]
        for oc in range(NCC):
            wi = wcol_load(woutD, oc, True)
            for ts in range(L // TSUB):
                a = t0 + ts * TSUB
                ao = a - WARM
                pp, ppk = psum()
                for kk in range(NCC):
                    off = kk * 2048 + ts * TSUB
                    S.add("pe", lambda E, p=pp, w_=WRG[wi], k2=kk, o=off,
                          s=(kk == 0), t=(kk == NCC - 1):
                          E.matmul(out=p[:], lhsT=w_[:, k2 * 128:(k2 + 1) * 128],
                                   rhs=RB[:, o:o + TSUB], start=s, stop=t),
                          r=[("wrg", wi)] + gr("RB", off, TSUB), w=[ppk])
                sx, sxk = STG[ts % 2], ("stg", ts % 2)
                dma("qs", sx[:], xR[:, oc, a:a + TSUB], w=[sxk])
                so, sok = STG[2 + ts % 2], ("stg", 2 + ts % 2)
                S.add("dve", lambda E, p=pp, i_=sx, o_=so: E.tensor_tensor(
                    out=o_[:], in0=p[:], in1=i_[:], op=ALU.add),
                    r=[ppk, sxk], w=[sok])
                dma("qs", partD[oc * 128:(oc + 1) * 128, ao:ao + TSUB], so[:],
                    r=[sok], w=[("dpart", oc, st, ts)])

    def hprime_stage(st):
        t0, L = STS[st]
        for cc in range(NCC):
            off = cc * 2048
            S.add("dve", lambda E, c=cc, o=off: E.tensor_scalar_mul(
                out=RA[:, o:o + 2048], in0=RA[:, o:o + 2048], scalar1=SUT[:, c:c + 1]),
                r=gr("RA", off, 2048) + [("SUT",)], w=gr("RA", off, 2048))
        for oc in range(NCC):
            wi = wcol_load(hD, oc, False)
            for ts in range(L // TSUB):
                pp, ppk = psum()
                for kk in range(NCC):
                    off = kk * 2048 + ts * TSUB
                    S.add("pe", lambda E, p=pp, w_=WRG[wi], k2=kk, o=off,
                          s=(kk == 0), t=(kk == NCC - 1):
                          E.matmul(out=p[:], lhsT=w_[:, k2 * 128:(k2 + 1) * 128],
                                   rhs=RA[:, o:o + TSUB], start=s, stop=t),
                          r=[("wrg", wi)] + gr("RA", off, TSUB), w=[ppk])
                off = oc * 2048 + ts * TSUB
                S.add("act", lambda E, p=pp, o=off: E.activation(
                    out=RB[:, o:o + TSUB], in_=p[:], func=AF.Identity, scale=SC1[:]),
                    r=[ppk, ("SC1",)], w=gr("RB", off, TSUB))

    def upgate_stage(st):
        t0, L = STS[st]
        for o in range(NOC):
            wu = wcol_load(wupD, o, True)
            wg = wcol_load(wgateD, o, True)
            for ts in range(L // TSUB):
                pu, puk = psum()
                pg, pgk = psum()
                for kk in range(NCC):
                    off = kk * 2048 + ts * TSUB
                    S.add("pe", lambda E, p=pu, w_=WRG[wu], k2=kk, of=off,
                          s=(kk == 0), t=(kk == NCC - 1):
                          E.matmul(out=p[:], lhsT=w_[:, k2 * 128:(k2 + 1) * 128],
                                   rhs=RB[:, of:of + TSUB], start=s, stop=t),
                          r=[("wrg", wu)] + gr("RB", off, TSUB), w=[puk])
                for kk in range(NCC):
                    off = kk * 2048 + ts * TSUB
                    S.add("pe", lambda E, p=pg, w_=WRG[wg], k2=kk, of=off,
                          s=(kk == 0), t=(kk == NCC - 1):
                          E.matmul(out=p[:], lhsT=w_[:, k2 * 128:(k2 + 1) * 128],
                                   rhs=RB[:, of:of + TSUB], start=s, stop=t),
                          r=[("wrg", wg)] + gr("RB", off, TSUB), w=[pgk])
                su, suk = STG[4], ("stg", 4)
                sg, sgk = STG[5], ("stg", 5)
                S.add("act", lambda E, p=pu, o_=su, oo=o: E.activation(
                    out=o_[:], in_=p[:], func=AF.Identity, scale=WSU[:, oo:oo + 1]),
                    r=[puk, ("WSU",)], w=[suk])
                S.add("act", lambda E, p=pg, o_=sg, oo=o: E.activation(
                    out=o_[:], in_=p[:], func=AF.Silu, scale=WSG[:, oo:oo + 1]),
                    r=[pgk, ("WSG",)], w=[sgk])
                gb, gbk = GBF[(o * 4 + ts) % 2], ("gbf", (o * 4 + ts) % 2)
                S.add("dve", lambda E, a_=sg, b_=su, g_=gb, oo=o: E.scalar_tensor_tensor(
                    out=g_[:], in0=a_[:], scalar=SUDs[:, oo:oo + 1], in1=b_[:],
                    op0=ALU.mult, op1=ALU.mult),
                    r=[sgk, suk, ("SUD",)], w=[gbk])
                blk = (st - 1) * 4 + ts
                dma("qs", gD[blk, :, o * 512:(o + 1) * 512], gb[:],
                    r=[gbk], w=[("dg", blk, o)])

    def down_stage(st):
        t0, L = STS[st]
        for ts in range(L // TSUB):
            blk = (st - 1) * 4 + ts
            ao = (t0 - WARM) + ts * TSUB
            dma("qs", RA[:], gD[blk, :, :],
                r=[("dg", blk, o) for o in range(NOC)],
                w=[("RA", g) for g in range(64)])
            for d in range(NCC):
                pp, ppk = psum()
                for og in range(4):
                    ki = og % 2
                    dma("qs", KST[ki][:], kfD[d, :, og * 2048:(og + 1) * 2048],
                        r=[("kf", d)], w=[("kst", ki)])
                    for j in range(16):
                        o = og * 16 + j
                        S.add("pe", lambda E, p=pp, k_=KST[ki], j2=j, o2=o,
                              s=(o == 0), t=(o == NOC - 1):
                              E.matmul(out=p[:], lhsT=k_[:, j2 * 128:(j2 + 1) * 128],
                                       rhs=RA[:, o2 * 512:(o2 + 1) * 512],
                                       start=s, stop=t),
                              r=[("kst", ki), ("RA", o)], w=[ppk])
                sp_, spk = STG[d % 2], ("stg", d % 2)
                dma("qs", sp_[:], partD[d * 128:(d + 1) * 128, ao:ao + TSUB],
                    r=[("dpart", d, st, ts)], w=[spk])
                so, sok = STG[2 + d % 2], ("stg", 2 + d % 2)
                S.add("dve", lambda E, p=pp, a_=sp_, o_=so, dd=d: E.scalar_tensor_tensor(
                    out=o_[:], in0=p[:], scalar=WSD[:, dd:dd + 1], in1=a_[:],
                    op0=ALU.mult, op1=ALU.add),
                    r=[ppk, spk, ("WSD",)], w=[sok])
                dma("qs", outD[d * 128:(d + 1) * 128, ao:ao + TSUB], so[:],
                    r=[sok], w=[("dout", d, st, ts)])

    def keff_prep():
        AEV = KST[0]
        CSTv = KST[1][0:64, :]
        dma("qs", H64s[:], h64D, w=[("h64",)])
        dma("qs", H128s[:], h128D, w=[("h128",)])
        for ch in range(64):
            wi = ch % 2
            dma("qg", WRG[wi][:], wdplD[:, ch * 2048:(ch + 1) * 2048], w=[("wrg", wi)])
            for j in range(4):
                pj, pjk = psum()
                S.add("pe", lambda E, p=pj, w_=WRG[wi], j2=j: E.matmul(
                    out=p[:], lhsT=H128s[:],
                    rhs=w_[:, j2 * 512:(j2 + 1) * 512], start=True, stop=True),
                    r=[("wrg", wi), ("h128",)], w=[pjk])
                S.add("act", lambda E, p=pj, j2=j: E.copy(
                    out=AEV[:, j2 * 512:(j2 + 1) * 512], in_=p[:]),
                    r=[pjk], w=[("kst", 0)])
            dma("qs", a1D[:, ch * 2048:(ch + 1) * 2048], AEV[:],
                r=[("kst", 0)], w=[("a1", ch)])
        a1R = a1D.rearrange("p (jhi d) -> p jhi d", jhi=64)
        kfR = kfD.rearrange("dch p (o c) -> dch p o c", o=NOC)
        for alo in range(128):
            dma("qs", CSTv, a1R[alo, :, :],
                r=[("a1", ch2) for ch2 in range(64)], w=[("kst", 1)])
            for j in range(4):
                pj, pjk = psum()
                S.add("pe", lambda E, p=pj, j2=j: E.matmul(
                    out=p[0:64, :], lhsT=H64s[:],
                    rhs=CSTv[:, j2 * 512:(j2 + 1) * 512], start=True, stop=True),
                    r=[("kst", 1), ("h64",)], w=[pjk])
                S.add("act", lambda E, p=pj, j2=j: E.copy(
                    out=AEV[0:64, j2 * 512:(j2 + 1) * 512], in_=p[0:64, :]),
                    r=[pjk], w=[("kst", 0)])
            srcb = AEV[0:64, 0:2048].rearrange("o (dch c) -> o dch c", dch=NCC)
            dstb = kfR[:, alo, :, :].rearrange("dch o c -> o dch c")
            dma("qs", dstb, srcb, r=[("kst", 0)], w=[("kfw", alo)])
        for d in range(NCC):
            S.add("pool", lambda E: E.engine_nop(),
                  r=[("kfw", alo) for alo in range(128)], w=[("kf", d)])

    ln_stage(0)
    wkv_stage(0)
    keff_prep()
    for st in (1, 2):
        ln_stage(st)
        wkv_stage(st)
        wout_stage(st)
        hprime_stage(st)
        upgate_stage(st)
        down_stage(st)

    S.finalize()

    eng_ops = {k: [] for k in ENG_ATTR}
    for op in S.ops:
        eng = Q_ENG[op["dma"]] if op["dma"] else op["eng"]
        eng_ops[eng].append(op)

    watermark = {}

    def emit(E, ename):
        for op in eng_ops[ename]:
            wl = {}
            for d in op["waits"]:
                p = S.ops[d]
                if p["sem"] is None:
                    continue
                if p["val"] > wl.get(p["sem"], 0):
                    wl[p["sem"]] = p["val"]
            for key, v in wl.items():
                wk = (ename, key)
                if watermark.get(wk, 0) >= v:
                    continue
                watermark[wk] = v
                E.wait_ge(sems[key], v)
            inst = op["build"](E)
            if op["dma"] is not None:
                inst.then_inc(sems[op["sem"]], 16)
            elif op["inc"]:
                inst.then_inc(sems[op["sem"]], 1)
        if ename == "sp":
            for q, n in Q_NSEM.items():
                for j in range(n):
                    fv = S.qfinal[q][j]
                    if fv > 0:
                        E.wait_ge(sems[(q, j)], fv)

    with nc.Block() as block:
        @block.tensor
        def _(E):
            emit(E, "pe")

        @block.vector
        def _(E):
            emit(E, "dve")

        @block.scalar
        def _(E):
            emit(E, "act")

        @block.gpsimd
        def _(E):
            emit(E, "pool")

        @block.sync
        def _(E):
            emit(E, "sp")

    st_.close()
    return nc


_NC_CACHE = [None]


def kernel(L0_attn_x, L0_attn_W_rvk, L0_attn_W_out, L0_attn_time_decay,
           L0_attn_time_first, L0_ffn_SU_ug, L0_ffn_SU_dp, L0_ffn_Wscale_ug,
           L0_ffn_Wscale_dp, L0_ffn_W_up, L0_ffn_W_gate, L0_ffn_W_down):
    x = np.asarray(L0_attn_x)
    f32 = np.float32
    bf16 = ml_dtypes.bfloat16

    def tiled(w, noc):
        return np.ascontiguousarray(
            np.asarray(w, f32).reshape(noc, 128, NCC, 128).transpose(0, 3, 2, 1))

    wrvk = tiled(L0_attn_W_rvk, 48)
    wup = tiled(L0_ffn_W_up, NOC)
    wgate = tiled(L0_ffn_W_gate, NOC)
    wout = tiled(L0_attn_W_out, NCC)
    Hm = sylvester(2048).astype(bf16)
    hmat = np.ascontiguousarray(Hm.reshape(NCC, 128, NCC, 128).transpose(0, 3, 2, 1))
    wdpl = np.ascontiguousarray(
        np.asarray(L0_ffn_W_down, f32).T.reshape(64, 128, 2048)
        .transpose(1, 0, 2).reshape(128, 64 * 2048))
    h64 = np.ascontiguousarray(sylvester(64).astype(bf16))
    h128 = np.ascontiguousarray(sylvester(128).astype(bf16))

    def colmaj(v, n):
        return np.ascontiguousarray(np.asarray(v, f32).reshape(n, 128).T)

    shared = dict(
        wrvk=wrvk, wup=wup, wgate=wgate, wout=wout, hmat=hmat, wdpl=wdpl,
        h64=h64, h128=h128,
        tdv=colmaj(L0_attn_time_decay, NCC), tfv=colmaj(L0_attn_time_first, NCC),
        suv=colmaj(L0_ffn_SU_ug, NCC),
        wsuv=colmaj(np.asarray(L0_ffn_Wscale_ug)[:8192], NOC),
        wsgv=colmaj(np.asarray(L0_ffn_Wscale_ug)[8192:], NOC),
        wsdv=colmaj(L0_ffn_Wscale_dp, NCC), sudv=colmaj(L0_ffn_SU_dp, NOC),
        onesb=np.ones((128, 1), dtype=bf16), onesr=np.ones((1, 128), dtype=f32))

    in_maps = []
    for c in range(NCORE):
        seq, t0 = c // 4, (c % 4) * TOK
        own = np.asarray(x[seq, t0:t0 + TOK, :], f32)
        if t0 == 0:
            warm = np.zeros((WARM, C), f32)
            wm = np.zeros((128, 1), f32)
        else:
            warm = np.asarray(x[seq, t0 - WARM:t0, :], f32)
            wm = np.ones((128, 1), f32)
        xc = np.ascontiguousarray(np.concatenate([warm, own], axis=0).T)
        m = dict(shared)
        m["x"] = xc
        m["wmask"] = wm
        in_maps.append(m)

    if _NC_CACHE[0] is None:
        _NC_CACHE[0] = build_nc()
    nc = _NC_CACHE[0]
    res = run_bass_kernel_spmd(nc, in_maps, core_ids=list(range(NCORE)))

    out = np.empty((B, T, C), f32)
    for c in range(NCORE):
        seq, t0 = c // 4, (c % 4) * TOK
        out[seq, t0:t0 + TOK, :] = np.asarray(res.results[c]["out"], f32).T
    return out
